# version 12
# baseline (speedup 1.0000x reference)
import math
from contextlib import ExitStack

import numpy as np
import ml_dtypes

import concourse.bass as bass
import concourse.mybir as mybir
from concourse.bass_utils import run_bass_kernel_spmd

F32 = mybir.dt.float32
BF16 = mybir.dt.bfloat16
AF = mybir.ActivationFunctionType
OP = mybir.AluOpType
AX = mybir.AxisListType

N_CORES = 8
D = 1024
DFF = 2048
E = 8
SEG = 4
H = 8
HD = 128
SCALE = 1.0 / math.sqrt(HD)
EPS = 1e-5

DEBUG_TAPS = False
PHASES = "AB"
MOE_EXPERTS = 8
MOE_LN2 = True
MOE_CHUNKS = 4
T = 4096
ACH = 512
N_ACH = T // ACH
MCH = 1024
N_MCH = T // MCH


class Plan:

    ENGS = ("sync", "tensor", "vector", "scalar", "gpsimd")

    def __init__(self):
        self.phases = [{e: [] for e in self.ENGS}]
        self.sem_names = []
        self.counts = {}
        self.last_tok = {}

    def sem(self, name):
        self.sem_names.append(name)
        self.counts[name] = 0
        return name

    def phase_break(self):
        self.phases.append({e: [] for e in self.ENGS})

    SELF_EDGE = ("vector", "scalar", "gpsimd")

    def op(self, eng, fn, deps=(), dma_sem=None):
        deps = [d for d in deps if d is not None]
        if dma_sem is not None:
            s, amt = dma_sem, 16
        else:
            s, amt = "s_" + eng, 1
            if eng in self.SELF_EDGE and eng in self.last_tok:
                deps.append(self.last_tok[eng])
        self.counts[s] += amt
        tok = (s, self.counts[s])
        self.phases[-1][eng].append((fn, tuple(deps), s, amt))
        if dma_sem is None:
            self.last_tok[eng] = tok
        return tok

    def check_feasible(self):
        vals = {s: 0 for s in self.counts}
        for ph in self.phases:
            idx = {e: 0 for e in self.ENGS}
            progress = True
            while progress:
                progress = False
                for e in self.ENGS:
                    while idx[e] < len(ph[e]):
                        fn, deps, s, amt = ph[e][idx[e]]
                        if all(vals[ds] >= dv for ds, dv in deps):
                            vals[s] += amt
                            idx[e] += 1
                            progress = True
                        else:
                            break
            if any(idx[e] < len(ph[e]) for e in self.ENGS):
                detail = []
                for e in self.ENGS:
                    if idx[e] < len(ph[e]):
                        fn, deps, s, amt = ph[e][idx[e]]
                        bad = [(ds, dv, vals[ds]) for ds, dv in deps if vals[ds] < dv]
                        detail.append(f"{e}[{idx[e]}] waits {bad}")
                raise RuntimeError("deadlock in plan: " + "; ".join(detail))

    def emit(self, nc):
        es = ExitStack()
        sems = {name: es.enter_context(nc.semaphore(name))
                for name in self.sem_names}
        for ph in self.phases:
            with nc.Block() as block:
                for engname in self.ENGS:
                    if not ph[engname]:
                        continue

                    def body(eng, stream=ph[engname]):
                        last_wait = {}
                        for fn, deps, s, amt in stream:
                            byname = {}
                            for ds, dv in deps:
                                byname[ds] = max(byname.get(ds, 0), dv)
                            for ds, dv in byname.items():
                                if last_wait.get(ds, -1) < dv:
                                    eng.wait_ge(sems[ds], dv)
                                    last_wait[ds] = dv
                            fn(eng, sems).then_inc(sems[s], amt)

                    getattr(block, engname)(body)
        return es


def build_program():
    nc = bass.Bass("TRN2", target_bir_lowering=False)
    P = Plan()
    for s in ("s_sync", "s_tensor", "s_vector", "s_scalar", "s_gpsimd"):
        P.sem(s)
    ld_cs = P.sem("ld_cs")
    ld_cw = P.sem("ld_cw")
    ld_aw = [P.sem(f"ld_aw{i}") for i in range(6)]
    ld_X = P.sem("ld_X")
    ld_xr = [P.sem("ld_xr0"), P.sem("ld_xr1")]
    ld_x1T = P.sem("ld_x1T")
    ld_w1 = [P.sem("ld_w1a"), P.sem("ld_w1b")]
    ld_w2 = [P.sem("ld_w2a"), P.sem("ld_w2b")]
    ld_w3 = P.sem("ld_w3")
    ld_xg = [P.sem("ld_xg0"), P.sem("ld_xg1")]
    st_f = [P.sem("st_f0"), P.sem("st_f1")]
    st_t = [P.sem("st_t0"), P.sem("st_t1")]
    st_o = [P.sem("st_o0"), P.sem("st_o1")]

    guard = {}

    def use(res, tok):
        guard.setdefault(res, []).append(tok)

    def take(res):
        return guard.pop(res, [])

    def vop(fn, deps=()):
        return P.op("vector", fn, list(deps))

    def sop(fn, deps=()):
        return P.op("scalar", fn, list(deps))

    def top(fn, deps=()):
        return P.op("tensor", fn, list(deps))

    def dma(eng, dst, src, deps=(), sem=None):
        return P.op(eng, lambda g, s, d=dst, sr=src: g.dma_start(d, sr),
                    deps=list(deps), dma_sem=sem)

    def din(name, shape, dt=F32):
        return nc.dram_tensor(name, shape, dt, kind="ExternalInput").ap()

    x_h = din("x", [T, D])
    qkwT_h = din("qkwT", [128, 8 * D]); vwT_h = din("vwT", [128, 8 * D])
    wqT_h = din("wqT", [128, 8 * D]); wkT_h = din("wkT", [128, 8 * D])
    wvT_h = din("wvT", [128, 8 * D])
    opT_h = din("opT", [128, 8 * D])
    qkb_c_h = din("qkb_c", [128, 8]); vb_c_h = din("vb_c", [128, 8])
    bq_r_h = din("bq_r", [1, D], BF16); bk_r_h = din("bk_r", [1, D], BF16)
    bv_r_h = din("bv_r", [1, D], BF16); opb_r_h = din("opb_r", [1, D], BF16)
    w1T_h = din("w1T", [E * 8 * 128, 8 * 256])
    w2T_h = din("w2T", [E * 8 * 128, 8 * 256])
    w3T_h = din("w3T", [E * 2 * 128, 8 * D])
    wgT_s_h = din("wgT_s", [128, 64])
    b1c_h = din("b1c", [128, E * 16]); b2c_h = din("b2c", [128, E * 16])
    b3_h = din("b3bf", [E, D], BF16)
    idnb_h = din("idnb", [128, 128], BF16); idnf_h = din("idnf", [128, 128])
    ones_h = din("onesb", [1, 512], BF16)

    out_h = nc.dram_tensor("out", [T, D], F32, kind="ExternalOutput").ap()
    dbg = {}
    if DEBUG_TAPS:
        for nm, shape, dt in (("dbg_X", [128, 4096], F32),
                              ("dbg_xT", [128, 4096], BF16),
                              ("dbg_qkin", [128, 4096], BF16),
                              ("dbg_vin", [128, 4096], BF16),
                              ("dbg_Q", [128, 4096], BF16),
                              ("dbg_K", [128, 4096], BF16),
                              ("dbg_V", [128, 4096], BF16),
                              ("dbg_S", [128, 96], F32),
                              ("dbg_A", [128, 96], F32),
                              ("dbg_o", [128, 1024], F32),
                              ("dbg_oT", [128, 1024], BF16),
                              ("dbg_xr", [128, 1024], F32)):
            dbg[nm] = nc.dram_tensor(nm, shape, dt, kind="ExternalOutput").ap()
    st_dbgs = {nm: P.sem(f"st_dbg{i}") for i, nm in enumerate(
        ("dbg_X", "dbg_xT", "dbg_qkin", "dbg_vin", "dbg_Q", "dbg_K", "dbg_V",
         "dbg_S", "dbg_A", "dbg_o", "dbg_oT", "dbg_xr"))}
    x1f_h = nc.dram_tensor("x1f", [T, D], F32, kind="Internal").ap()
    x1t_h = nc.dram_tensor("x1t", [8, 128, T], BF16, kind="Internal").ap()

    es_glob = ExitStack()

    def sbg(name, shape, dt):
        return es_glob.enter_context(nc.sbuf_tensor(name, shape, dt))

    idnb = sbg("idnb_s", [128, 128], BF16)
    idnf = sbg("idnf_s", [128, 128], F32)
    ones = sbg("ones_s", [1, 512], BF16)
    qkb_c = sbg("qkb_c_s", [128, 8], F32)
    vb_c = sbg("vb_c_s", [128, 8], F32)
    bq_r = sbg("bq_r_s", [1, D], BF16)
    bk_r = sbg("bk_r_s", [1, D], BF16)
    bv_r = sbg("bv_r_s", [1, D], BF16)
    opb_r = sbg("opb_r_s", [1, D], BF16)
    wgT_s = sbg("wgT_s_s", [128, 64], BF16)
    b1c = sbg("b1c_s", [128, E * 16], F32)
    b2c = sbg("b2c_s", [128, E * 16], F32)
    b3s = sbg("b3s_s", [E, D], BF16)

    c_toks = []
    for dst, src_ in ((idnb, idnb_h), (idnf, idnf_h), (ones, ones_h),
                      (bq_r, bq_r_h), (bk_r, bk_r_h), (bv_r, bv_r_h),
                      (opb_r, opb_r_h), (b3s, b3_h)):
        c_toks.append(dma("sync", dst[:, :], src_, sem=ld_cs))
    for dst, src_ in ((qkb_c, qkb_c_h), (vb_c, vb_c_h), (wgT_s, wgT_s_h),
                      (b1c, b1c_h), (b2c, b2c_h)):
        c_toks.append(dma("gpsimd", dst[:, :], src_, sem=ld_cw))
    CONST = [c_toks[-6], c_toks[-1]]

    es_att = ExitStack()

    def sba(name, shape, dt):
        return es_att.enter_context(nc.sbuf_tensor(name, shape, dt))

    qkwT = sba("qkwT_s", [128, 8 * D], BF16)
    vwT = sba("vwT_s", [128, 8 * D], BF16)
    wqT = sba("wqT_s", [128, 8 * D], BF16)
    wkT = sba("wkT_s", [128, 8 * D], BF16)
    wvT = sba("wvT_s", [128, 8 * D], BF16)
    opT = sba("opT_s", [128, 8 * D], BF16)
    Xbf = sba("Xbf", [128, 4 * D], BF16)
    xres = [sba(f"xres{p}", [128, D], F32) for p in range(2)]
    xT = sba("xT", [128, 8 * ACH], BF16)
    qk_in = sba("qk_in", [128, 8 * ACH], BF16)
    v_in = sba("v_in", [128, 8 * ACH], BF16)
    Q = sba("Q", [128, 4 * D], BF16)
    K = sba("K", [128, 4 * D], BF16)
    V = sba("V", [128, 4 * D], BF16)
    Pb = sba("Pb", [128, D], F32)
    Sb = sba("Sb", [128, 96], F32)
    Ebuf = sba("Ebuf", [128, 96], F32)
    Db = sba("Db", [128, 32], F32)
    Rb = sba("Rb", [128, 32], F32)
    Ab = sba("Ab", [128, 96], F32)
    ob = [sba(f"ob{p}", [128, D], F32) for p in range(2)]
    obf = [sba(f"obf{p}", [128, D], BF16) for p in range(2)]
    oT = [sba(f"oT{p}", [128, D], BF16) for p in range(2)]
    xr = [sba(f"xr{p}", [128, D], F32) for p in range(2)]
    x1b = ob
    x1bf = [sba(f"x1bf{p}", [128, D], BF16) for p in range(2)]
    x1ts = obf
    scol = sba("scol", [128, 1], F32)
    negm = sba("negm", [128, 1], F32)
    vars_ = sba("vars", [128, 1], F32)
    v1c = sba("v1c", [128, 1], F32)
    l1c = sba("l1c", [128, 1], F32)
    rstd = sba("rstd", [128, 1], F32)

    pband = [es_att.enter_context(nc.psum_tensor(f"pb{i}", [128, 512], F32))
             for i in range(6)]
    trb = [es_att.enter_context(nc.psum_tensor(f"trb{i}", [128, 1024], BF16))
           for i in range(2)]

    xsrc0 = bass.AP(x_h.tensor, 0, [[D, 128], [128 * D, 4], [1, D]])
    t_xl0 = dma("gpsimd", Xbf[:, :], xsrc0, sem=ld_X)
    aw = []
    for wi, (dst, src_) in enumerate(((qkwT, qkwT_h), (vwT, vwT_h),
                                      (wqT, wqT_h), (wkT, wkT_h),
                                      (wvT, wvT_h), (opT, opT_h))):
        aw.append(dma("gpsimd", dst[:, :], src_, sem=ld_aw[wi]))

    pairs = [(i, jj, [j for j in range(4) if j != i][jj])
             for i in range(4) for jj in range(3)]

    def pcol(i, jj):
        return (i * 3 + jj) * 8

    x1f_sp = []
    x1t_sp = []

    def emit_X1(c):
        if c == 0:
            t_xl = t_xl0
        else:
            xsrc = bass.AP(x_h.tensor, c * ACH * D,
                           [[D, 128], [128 * D, 4], [1, D]])
            t_xl = dma("gpsimd", Xbf[:, :], xsrc, deps=take("Xbf"), sem=ld_X)

        xT_evacs = []
        xT_guard = take("xT")
        last_tr = None
        for i in range(4):
            bk = trb[0]
            bankdeps = [t_xl] + take("tr0") + (CONST if c == 0 and i == 0 else [])
            ttr = None
            for dt in range(8):
                def f(t, s, i=i, dt=dt, bk=bk):
                    return t.transpose(bk[:, dt * 128:(dt + 1) * 128],
                                       Xbf[:, i * D + dt * 128: i * D + (dt + 1) * 128],
                                       idnb[:, :])
                ttr = top(f, bankdeps)
                bankdeps = []
            last_tr = ttr

            def fe(se, s, i=i, bk=bk):
                dst = bass.AP(xT, i * 128, [[8 * ACH, 128], [ACH, 8], [1, 128]])
                src = bass.AP(bk, 0, [[1024, 128], [128, 8], [1, 128]])
                return se.activation(dst, src, AF.Identity)
            tev = sop(fe, [ttr] + xT_guard)
            xT_guard = []
            use("tr0", tev)
            xT_evacs.append(tev)
        use("Xbf", last_tr)

        def proj_fm(wsb, bcols, dst_sb, wtok, wr_guard):
            evacs = []
            for dt in range(8):
                bname = f"pb{dt % 2}"
                bk = pband[dt % 2]
                mmdeps = xT_evacs + [wtok] + take(bname)
                tmm = None
                for kt in range(8):
                    def f(t, s, dt=dt, kt=kt, bk=bk, wsb=wsb):
                        return t.matmul(
                            bk[:, :],
                            wsb[:, kt * D + dt * 128: kt * D + (dt + 1) * 128],
                            xT[:, kt * ACH: (kt + 1) * ACH],
                            start=(kt == 0), stop=(kt == 7))
                    tmm = top(f, mmdeps)
                    mmdeps = []
                use("xT", tmm)

                def fe(se, s, dt=dt, bk=bk, dst_sb=dst_sb, bcols=bcols):
                    return se.activation(dst_sb[:, dt * ACH:(dt + 1) * ACH],
                                         bk[:, :], AF.Identity,
                                         bias=bcols[:, dt:dt + 1])
                ev = sop(fe, [tmm] + wr_guard)
                use(bname, ev)
                evacs.append(ev)
            return evacs

        qk_ev = proj_fm(qkwT, qkb_c, qk_in, aw[0], take("qk_in"))
        v_ev = proj_fm(vwT, vb_c, v_in, aw[1], take("v_in"))
        return {"qk_ev": qk_ev, "v_ev": v_ev}

    def emit_X2_block(c, st, name, i, qkv_ev):
        cfg = {"q": (qk_in, wqT, bq_r, aw[2], st["qk_ev"], (2, 3)),
               "k": (qk_in, wkT, bk_r, aw[3], st["qk_ev"], (4, 5)),
               "v": (v_in, wvT, bv_r, aw[4], st["v_ev"], (2, 3))}
        src_sb, wsb, brow, wtok, src_ev, banks = cfg[name]
        dst = {"q": Q, "k": K, "v": V}[name]
        evs = []
        mmdeps = list(src_ev) + [wtok] + take(f"pb{banks[0]}") \
            + take(f"pb{banks[1]}")
        for kt in range(8):
            for half in range(2):
                def f(t, s, i=i, kt=kt, half=half,
                      src_sb=src_sb, wsb=wsb, bk=pband[banks[half]]):
                    return t.matmul(
                        bk[:, :],
                        src_sb[:, kt * ACH + i * 128: kt * ACH + i * 128 + 128],
                        wsb[:, kt * D + half * 512: kt * D + half * 512 + 512],
                        start=(kt == 0), stop=False)
                top(f, mmdeps)
                mmdeps = []
        for half in range(2):
            def fb(t, s, half=half, bk=pband[banks[half]], brow=brow):
                return t.matmul(bk[:, :], ones[0:1, 0:128],
                                brow[0:1, half * 512: half * 512 + 512],
                                start=False, stop=True)
            tmm = top(fb)
            use("qk_in" if name in ("q", "k") else "v_in", tmm)

            def fe(se, s, i=i, half=half, bk=pband[banks[half]], dst=dst):
                return se.activation(
                    dst[:, i * D + half * 512: i * D + half * 512 + 512],
                    bk[:, :], AF.Identity)
            ev = sop(fe, [tmm] + take(f"{name}w"))
            use(f"pb{banks[half]}", ev)
            evs.append(ev)
        qkv_ev[(name, i)] = evs

    def emit_Y_scores(c, qkv_ev):
        s_last = None
        for (i, jj, j) in pairs:
            def fmul(v, s, i=i, j=j):
                return v.tensor_tensor(Pb[:, :], Q[:, i * D:(i + 1) * D],
                                       K[:, j * D:(j + 1) * D], OP.mult)
            tm = vop(fmul, qkv_ev[("q", i)] + qkv_ev[("k", j)])
            use("qw", tm)
            use("kw", tm)

            def fred(v, s, i=i, jj=jj):
                src = bass.AP(Pb, 0, [[D, 128], [128, 8], [1, 128]])
                return v.tensor_reduce(Sb[:, pcol(i, jj):pcol(i, jj) + 8], src,
                                       AX.X, OP.add)
            s_last = vop(fred, [tm])
        t_exp = sop(lambda se, s: se.activation(Ebuf[:, :], Sb[:, :], AF.Exp,
                                                scale=SCALE), [s_last])
        d_last = None
        for i in range(4):
            def f1(v, s, i=i):
                return v.tensor_tensor(Db[:, i * 8:(i + 1) * 8],
                                       Ebuf[:, pcol(i, 0):pcol(i, 0) + 8],
                                       Ebuf[:, pcol(i, 1):pcol(i, 1) + 8], OP.add)
            vop(f1, [t_exp])

            def f2(v, s, i=i):
                return v.tensor_tensor(Db[:, i * 8:(i + 1) * 8],
                                       Db[:, i * 8:(i + 1) * 8],
                                       Ebuf[:, pcol(i, 2):pcol(i, 2) + 8], OP.add)
            d_last = vop(f2)
        t_rec = vop(lambda v, s: v.reciprocal(Rb[:, :], Db[:, :]), [d_last])
        a_tok = {}
        for (i, jj, j) in pairs:
            def fa(v, s, i=i, jj=jj):
                return v.tensor_tensor(Ab[:, pcol(i, jj):pcol(i, jj) + 8],
                                       Ebuf[:, pcol(i, jj):pcol(i, jj) + 8],
                                       Rb[:, i * 8:(i + 1) * 8], OP.mult)
            a_tok[(i, jj)] = vop(fa, [t_rec])
        return a_tok

    def emit_Y_grp(c, i, qkv_ev, a_tok):
        if True:
            par = i % 2
            obuf = ob[par]
            lastv = None
            first_deps = take(f"ob{par}")
            for jj in range(3):
                j = [jt for jt in range(4) if jt != i][jj]
                for h in range(H):
                    acol = Ab[:, pcol(i, jj) + h: pcol(i, jj) + h + 1]
                    vsl = V[:, j * D + h * 128: j * D + (h + 1) * 128]
                    osl = obuf[:, h * 128:(h + 1) * 128]
                    deps = [a_tok[(i, jj)]] + qkv_ev[("v", j)] + first_deps
                    first_deps = []
                    if jj == 0:
                        def f(v, s, osl=osl, vsl=vsl, acol=acol):
                            return v.tensor_scalar(osl, vsl, acol, None, OP.mult)
                    else:
                        def f(v, s, osl=osl, vsl=vsl, acol=acol):
                            return v.scalar_tensor_tensor(osl, vsl, acol, osl,
                                                          OP.mult, OP.add)
                    lastv = vop(f, deps)
            o_done_i = lastv
            use("vw", o_done_i)

            t_obf = vop(lambda v, s, par=par: v.tensor_copy(obf[par][:, :],
                                                            ob[par][:, :]),
                        [o_done_i] + take(f"obf{par}"))
            bk = trb[1]
            tdeps = [t_obf] + take("tr1")
            ttr = None
            for dt in range(8):
                def f(t, s, par=par, dt=dt, bk=bk):
                    return t.transpose(bk[:, dt * 128:(dt + 1) * 128],
                                       obf[par][:, dt * 128:(dt + 1) * 128],
                                       idnb[:, :])
                ttr = top(f, tdeps)
                tdeps = []
            use(f"obf{par}", ttr)
            t_oT = vop(lambda v, s, par=par, bk=bk: v.tensor_copy(oT[par][:, :],
                                                                  bk[:, :]),
                       [ttr] + take(f"oT{par}"))
            use("tr1", t_oT)

            vt0 = c * ACH + i * 128
            t_xres = dma("sync", xres[par][:, :],
                         bass.AP(x_h.tensor, vt0 * D, [[D, 128], [1, D]]),
                         deps=take(f"xres{par}"), sem=ld_xr[par])
            xr_parts = []
            for half in range(2):
                bname2 = f"pb{half}"
                bk2 = pband[half]
                mmdeps = [t_oT, aw[5]] + take(bname2)
                tmm = None
                for kt in range(8):
                    def f(t, s, kt=kt, half=half, bk2=bk2, par=par):
                        return t.matmul(
                            bk2[:, :],
                            oT[par][:, kt * 128:(kt + 1) * 128],
                            opT[:, kt * D + half * 512: kt * D + half * 512 + 512],
                            start=(kt == 0), stop=False)
                    tmm = top(f, mmdeps)
                    mmdeps = []

                def fb(t, s, half=half, bk2=bk2):
                    return t.matmul(bk2[:, :], ones[0:1, 0:128],
                                    opb_r[0:1, half * 512: half * 512 + 512],
                                    start=False, stop=True)
                tmm = top(fb)
                use(f"oT{par}", tmm)

                def fxr(v, s, par=par, half=half, bk2=bk2):
                    return v.tensor_tensor(
                        xr[par][:, half * 512: half * 512 + 512],
                        xres[par][:, half * 512: half * 512 + 512],
                        bk2[:, :], OP.add)
                tx = vop(fxr, [tmm, t_xres]
                         + take(f"xr{par}" if half == 0 else "_n"))
                use(bname2, tx)
                use(f"xres{par}", tx)
                xr_parts.append(tx)

            t_s = sop(lambda se, s, par=par: se.activation(
                obf[par][:, :], xr[par][:, :], AF.Identity,
                accum_out=scol[:, 0:1]), xr_parts)
            t_nm = vop(lambda v, s: v.tensor_scalar(negm[:, :], scol[:, :],
                                                    -1.0 / D, None, OP.mult),
                       [t_s])
            t_v = sop(lambda se, s, par=par: se.activation(
                obf[par][:, :], xr[par][:, :], AF.Square, bias=negm[:, 0:1],
                accum_out=vars_[:, 0:1]), [t_nm])
            use(f"xr{par}", t_v)
            t_v1 = vop(lambda v, s: v.tensor_scalar(v1c[:, :], vars_[:, :],
                                                    1.0 / D, EPS, OP.mult, OP.add),
                       [t_v])
            t_l = sop(lambda se, s: se.activation(l1c[:, :], v1c[:, :], AF.Ln),
                      [t_v1])
            t_r = sop(lambda se, s: se.activation(rstd[:, :], l1c[:, :], AF.Exp,
                                                  scale=-0.5), [t_l])
            t_x1 = vop(lambda v, s, par=par: v.tensor_scalar(
                x1b[par][:, :], xr[par][:, :], negm[:, 0:1], rstd[:, 0:1],
                OP.add, OP.mult), [t_r] + take(f"ob{par}"))
            use(f"xr{par}", t_x1)
            t_x1bf = vop(lambda v, s, par=par: v.tensor_copy(x1bf[par][:, :],
                                                             x1b[par][:, :]),
                         [t_x1] + take(f"x1bf{par}"))
            t_sp1 = dma("sync", bass.AP(x1f_h.tensor, vt0 * D, [[D, 128], [1, D]]),
                        x1b[par][:, :], deps=[t_x1], sem=st_f[par])
            use(f"ob{par}", t_sp1)
            x1f_sp.append(t_sp1)

            bk3 = trb[1]
            tdeps = [t_x1bf] + take("tr1")
            ttr = None
            for dt in range(8):
                def f(t, s, par=par, dt=dt, bk3=bk3):
                    return t.transpose(bk3[:, dt * 128:(dt + 1) * 128],
                                       x1bf[par][:, dt * 128:(dt + 1) * 128],
                                       idnb[:, :])
                ttr = top(f, tdeps)
                tdeps = []
            use(f"x1bf{par}", ttr)
            t_xe = vop(lambda v, s, par=par, bk3=bk3: v.tensor_copy(
                x1ts[par][:, :], bk3[:, :]), [ttr])
            use("tr1", t_xe)
            dst = bass.AP(x1t_h.tensor, vt0, [[T, 128], [128 * T, 8], [1, 128]])
            src = bass.AP(x1ts[par], 0, [[D, 128], [128, 8], [1, 128]])
            t_sp2 = dma("sync", dst, src, deps=[t_xe], sem=st_t[par])
            use(f"obf{par}", t_sp2)
            x1t_sp.append(t_sp2)

    stX1 = emit_X1(0)
    qkv_cur = {}
    for i in range(4):
        for nm in ("q", "k", "v"):
            emit_X2_block(0, stX1, nm, i, qkv_cur)
    for c in range(N_ACH):
        stX1n = emit_X1(c + 1) if c + 1 < N_ACH else None
        a_tok = emit_Y_scores(c, qkv_cur)
        qkv_nxt = {}
        for i in range(4):
            if stX1n is not None:
                emit_X2_block(c + 1, stX1n, "q", i, qkv_nxt)
                emit_X2_block(c + 1, stX1n, "k", i, qkv_nxt)
            emit_Y_grp(c, i, qkv_cur, a_tok)
        if stX1n is not None:
            for i in range(4):
                emit_X2_block(c + 1, stX1n, "v", i, qkv_nxt)
        qkv_cur = qkv_nxt

    att_done = [(st_f[0], P.counts[st_f[0]]), (st_f[1], P.counts[st_f[1]]),
                (st_t[0], P.counts[st_t[0]]), (st_t[1], P.counts[st_t[1]])]

    es_att.close()
    if PHASES == "A":
        P.op("sync", lambda sy, s: sy.nop(),
             deps=[(st_f[0], P.counts[st_f[0]]), (st_f[1], P.counts[st_f[1]]),
                   (st_t[0], P.counts[st_t[0]]), (st_t[1], P.counts[st_t[1]])])
        P.check_feasible()
        sem_es = P.emit(nc)
        sem_es.close()
        es_glob.close()
        return nc
    P.phase_break()
    guard.clear()

    es_moe = ExitStack()

    def sbm(name, shape, dt):
        return es_moe.enter_context(nc.sbuf_tensor(name, shape, dt))

    x1T = sbm("x1T", [128, 8 * MCH], BF16)
    hsb = sbm("hsb", [128, 16 * MCH], BF16)
    acc2 = [sbm(f"acc{p}", [128, 8 * D], F32) for p in range(2)]
    w3sb = sbm("w3sb", [128, 16 * D], BF16)
    w1f = [sbm(f"w1f{p}", [128, 8 * 256], BF16) for p in range(2)]
    w2f = [sbm(f"w2f{p}", [128, 8 * 256], BF16) for p in range(2)]
    su = [sbm(f"su{p}", [128, MCH], BF16) for p in range(2)]
    gout = sbm("gout", [128, 8], F32)
    m1 = sbm("m1", [128, 1], F32)
    m2 = sbm("m2", [128, 1], F32)
    ind1 = sbm("ind1", [128, 8], F32)
    ind2 = sbm("ind2", [128, 8], F32)
    maskg = sbm("maskg", [128, 8], F32)
    diffg = sbm("diffg", [128, 1], F32)
    w1cg = sbm("w1cg", [128, 1], F32)
    w2cg = sbm("w2cg", [128, 1], F32)
    comb = sbm("comb", [128, 8 * E], F32)
    combT = sbm("combT", [8, 8 * 128], BF16)
    x1g = [sbm(f"x1g{p}", [128, D], F32) for p in range(2)]
    xr2 = [sbm(f"xr2{p}", [128, D], F32) for p in range(2)]
    outg = [sbm(f"outg{p}", [128, D], F32) for p in range(2)]
    junk2 = sbm("junk2", [128, D], BF16)
    sc2 = sbm("sc2", [128, 1], F32)
    nm2 = sbm("nm2", [128, 1], F32)
    vr2 = sbm("vr2", [128, 1], F32)
    v12 = sbm("v12", [128, 1], F32)
    l12 = sbm("l12", [128, 1], F32)
    rs2 = sbm("rs2", [128, 1], F32)

    pm = [es_moe.enter_context(nc.psum_tensor(f"pm{i}", [128, 512], F32))
          for i in range(8)]

    out_toks = []
    nchunk = min(N_MCH, MOE_CHUNKS)

    def emit_x1T_reload(k):
        toks = []
        g1T = (att_done if k == 0 else []) + take("x1T")
        for dt in range(8):
            src = bass.AP(x1t_h.tensor, dt * 128 * T + k * MCH,
                          [[T, 128], [1, MCH]])
            toks.append(dma("sync", x1T[:, dt * MCH:(dt + 1) * MCH], src,
                            deps=g1T, sem=ld_x1T))
        return toks

    def _emit_LN2_body(k, acc_tok, comb_tok):
        for j in range(8 if MOE_LN2 else 0):
            g = k * 8 + j
            par = j % 2
            vt0 = g * 128
            src = bass.AP(x1f_h.tensor, vt0 * D, [[D, 128], [1, D]])
            t_xg = dma("sync", x1g[par][:, :], src,
                       deps=att_done[:2] + take(f"x1g{par}"), sem=ld_xg[par])
            t_xr = vop(lambda v, s, j=j, par=par, k=k: v.tensor_tensor(
                xr2[par][:, :], x1g[par][:, :],
                acc2[k % 2][:, j * D:(j + 1) * D], OP.add),
                [t_xg, acc_tok[(j, 0)], acc_tok[(j, 1)]] + take(f"xr2{par}"))
            use(f"x1g{par}", t_xr)
            use(f"combg{j}", t_xr)
            t_s = sop(lambda se, s, par=par: se.activation(
                junk2[:, :], xr2[par][:, :], AF.Identity, accum_out=sc2[:, 0:1]),
                [t_xr])
            t_nm = vop(lambda v, s: v.tensor_scalar(nm2[:, :], sc2[:, :],
                                                    -1.0 / D, None, OP.mult),
                       [t_s])
            t_v = sop(lambda se, s, par=par: se.activation(
                junk2[:, :], xr2[par][:, :], AF.Square, bias=nm2[:, 0:1],
                accum_out=vr2[:, 0:1]), [t_nm])
            use(f"xr2{par}", t_v)
            t_v1 = vop(lambda v, s: v.tensor_scalar(v12[:, :], vr2[:, :],
                                                    1.0 / D, EPS, OP.mult,
                                                    OP.add), [t_v])
            t_l = sop(lambda se, s: se.activation(l12[:, :], v12[:, :], AF.Ln),
                      [t_v1])
            t_r = sop(lambda se, s: se.activation(rs2[:, :], l12[:, :], AF.Exp,
                                                  scale=-0.5), [t_l])
            t_o = vop(lambda v, s, par=par: v.tensor_scalar(
                outg[par][:, :], xr2[par][:, :], nm2[:, 0:1], rs2[:, 0:1],
                OP.add, OP.mult), [t_r] + take(f"outg{par}"))
            use(f"xr2{par}", t_o)
            dst = bass.AP(out_h.tensor, vt0 * D, [[D, 128], [1, D]])
            t_st = dma("sync", dst, outg[par][:, :], deps=[t_o], sem=st_o[par])
            use(f"outg{par}", t_st)
            out_toks.append(t_st)
        for j in range(8):
            use(f"combTg{j}", acc_tok[(j, 0)])


    pend_LN2 = []
    x1T_next = emit_x1T_reload(0)

    for k in range(nchunk):
        x1T_toks = x1T_next

        comb_tok = [None] * 8
        acc_tok = {}
        for j in range(8):
            mmdeps = x1T_toks + CONST + take("pm6")
            tmm = None
            for kt in range(8):
                def f(t, s, j=j, kt=kt):
                    return t.matmul(
                        pm[6][:, 0:8],
                        x1T[:, kt * MCH + j * 128: kt * MCH + j * 128 + 128],
                        wgT_s[:, kt * 8:(kt + 1) * 8],
                        start=(kt == 0), stop=(kt == 7))
                tmm = top(f, mmdeps)
                mmdeps = []
            t_g = vop(lambda v, s: v.tensor_copy(gout[:, :], pm[6][:, 0:8]),
                      [tmm])
            use("pm6", t_g)
            t_m1 = vop(lambda v, s: v.reduce_max(m1[:, :], gout[:, :], AX.X))
            t_i1 = vop(lambda v, s: v.tensor_scalar(ind1[:, :], gout[:, :],
                                                    m1[:, 0:1], None, OP.is_ge))
            t_mk = vop(lambda v, s: v.scalar_tensor_tensor(
                maskg[:, :], ind1[:, :], -1e30, gout[:, :], OP.mult, OP.add))
            t_m2 = vop(lambda v, s: v.reduce_max(m2[:, :], maskg[:, :], AX.X))
            t_i2 = vop(lambda v, s: v.tensor_scalar(ind2[:, :], maskg[:, :],
                                                    m2[:, 0:1], None, OP.is_ge))
            t_df = vop(lambda v, s: v.tensor_tensor(diffg[:, :], m2[:, :],
                                                    m1[:, :], OP.subtract))
            t_w2 = sop(lambda se, s: se.activation(w2cg[:, :], diffg[:, :],
                                                   AF.Sigmoid), [t_df])
            t_w1 = sop(lambda se, s: se.activation(w1cg[:, :], diffg[:, :],
                                                   AF.Sigmoid, scale=-1.0))
            t_c1 = vop(lambda v, s, j=j: v.tensor_scalar(
                comb[:, j * 8:(j + 1) * 8], ind1[:, :], w1cg[:, 0:1], None,
                OP.mult), [t_w1] + take(f"combg{j}"))
            t_cb = vop(lambda v, s, j=j: v.scalar_tensor_tensor(
                comb[:, j * 8:(j + 1) * 8], ind2[:, :], w2cg[:, 0:1],
                comb[:, j * 8:(j + 1) * 8], OP.mult, OP.add))
            comb_tok[j] = t_cb

        def emit_gating_part2():
            for j in range(8):
                def ftr(t, s, j=j):
                    return t.transpose(pm[7][0:8, 0:128],
                                       comb[:, j * 8:(j + 1) * 8], idnf[:, :])
                ttr = top(ftr, [comb_tok[j]] + take("pm7"))

                def fce(v, s, j=j):
                    return v.tensor_copy(combT[:, j * 128:(j + 1) * 128],
                                         pm[7][0:8, 0:128])
                t_ct = vop(fce, [ttr] + take(f"combTg{j}"))
                use("pm7", t_ct)
                for half in range(2):
                    bn = f"pm{half}"

                    def fmm(t, s, j=j, half=half):
                        return t.matmul(pm[half][:, :],
                                        combT[:, j * 128:(j + 1) * 128],
                                        b3s[:, half * 512: half * 512 + 512],
                                        start=True, stop=True)
                    tmm2 = top(fmm, [t_ct] + take(bn))
                    use(f"combTg{j}", tmm2)

                    def fai(v, s, j=j, half=half, k=k):
                        return v.tensor_copy(
                            acc2[k % 2][:, j * D + half * 512:
                                        j * D + half * 512 + 512],
                            pm[half][:, :])
                    t_ai = vop(fai, [tmm2])
                    use(bn, t_ai)
                    acc_tok[(j, half)] = t_ai

        for e in range(MOE_EXPERTS):
            def w_slice_dma(sl, e=e, extra=()):
                off_ = (e * 8 + sl) * 128 * 2048
                src1 = bass.AP(w1T_h.tensor, off_, [[2048, 128], [1, 2048]])
                src2 = bass.AP(w2T_h.tensor, off_, [[2048, 128], [1, 2048]])
                t1 = dma("gpsimd", w1f[sl % 2][:, :], src1,
                         deps=take(f"w1f{sl % 2}") + list(extra),
                         sem=ld_w1[sl % 2])
                t2 = dma("gpsimd", w2f[sl % 2][:, :], src2,
                         deps=take(f"w2f{sl % 2}"), sem=ld_w2[sl % 2])
                return t1, t2

            w1_toks = {}
            w2_toks = {}
            extra0 = att_done if (k == 0 and e == 0) else []
            w1_toks[0], w2_toks[0] = w_slice_dma(0, extra=extra0)
            w1_toks[1], w2_toks[1] = w_slice_dma(1)
            w3_toks = []
            g3 = take("w3sb")
            for hh in range(2):
                src3 = bass.AP(w3T_h.tensor, (e * 2 + hh) * 128 * 8 * D,
                               [[8 * D, 128], [1, 8 * D]])
                w3_toks.append(dma("gpsimd",
                                   w3sb[:, hh * 8 * D:(hh + 1) * 8 * D],
                                   src3, deps=g3, sem=ld_w3))

            h_last = None
            for f in range(16):
                sl = f // 2
                off = (f % 2) * 128
                gbanks = (0, 1) if f % 2 == 0 else (4, 5)
                vbanks = (2, 3) if f % 2 == 0 else (6, 7)
                mmdeps = [w1_toks[sl]] + (x1T_toks if f == 0 and e == 0 else []) \
                    + take(f"pm{gbanks[0]}") + take(f"pm{gbanks[1]}") \
                    + take(f"hsb_f{f}")
                tmm = None
                for kt in range(8):
                    for half in range(2):
                        def f_(t, s, kt=kt, half=half, gb=pm[gbanks[half]],
                               sl=sl, off=off):
                            return t.matmul(
                                gb[:, :],
                                w1f[sl % 2][:, kt * 256 + off: kt * 256 + off + 128],
                                x1T[:, kt * MCH + half * 512: kt * MCH + half * 512 + 512],
                                start=(kt == 0), stop=(kt == 7))
                        tmm = top(f_, mmdeps)
                        mmdeps = []
                se_toks = []
                for half in range(2):
                    def fs(se, s, e=e, f=f, half=half, gb=pm[gbanks[half]],
                           fpar=f % 2):
                        return se.activation(
                            su[fpar][:, half * 512: half * 512 + 512],
                            gb[:, :], AF.Silu,
                            bias=b1c[:, e * 16 + f: e * 16 + f + 1])
                    se_tok = sop(fs, [tmm] + take(f"su{f % 2}_{half}"))
                    use(f"pm{gbanks[half]}", se_tok)
                    se_toks.append(se_tok)
                mmdeps = [w2_toks[sl]] + take(f"pm{vbanks[0]}") \
                    + take(f"pm{vbanks[1]}")
                tmmv = None
                for kt in range(8):
                    for half in range(2):
                        def f_(t, s, kt=kt, half=half, vb=pm[vbanks[half]],
                               sl=sl, off=off):
                            return t.matmul(
                                vb[:, :],
                                w2f[sl % 2][:, kt * 256 + off: kt * 256 + off + 128],
                                x1T[:, kt * MCH + half * 512: kt * MCH + half * 512 + 512],
                                start=(kt == 0), stop=(kt == 7))
                        tmmv = top(f_, mmdeps)
                        mmdeps = []
                for half in range(2):
                    def fh(v, s, e=e, f=f, half=half, vb=pm[vbanks[half]],
                           fpar=f % 2):
                        return v.scalar_tensor_tensor(
                            hsb[:, f * MCH + half * 512: f * MCH + half * 512 + 512],
                            vb[:, :], b2c[:, e * 16 + f: e * 16 + f + 1],
                            su[fpar][:, half * 512: half * 512 + 512],
                            OP.add, OP.mult)
                    th = vop(fh, [tmmv, se_toks[half]])
                    use(f"pm{vbanks[half]}", th)
                    use(f"su{f % 2}_{half}", th)
                    h_last = th
                if f == 2 * sl + 1:
                    use(f"w1f{sl % 2}", h_last)
                    use(f"w2f{sl % 2}", h_last)
                    if sl + 2 < 8:
                        w1_toks[sl + 2], w2_toks[sl + 2] = w_slice_dma(sl + 2)
                if f == 15 and e == E - 1:
                    use("x1T", h_last)

            if e == 0:
                emit_gating_part2()
                while pend_LN2:
                    pend_LN2.pop(0)()

            last_mm = None
            for j in range(8):
                banks = (0, 1) if j % 2 == 0 else (2, 3)
                for half in range(2):
                    bk = pm[banks[half]]
                    bn = f"pm{banks[half]}"
                    mmdeps = [h_last, w3_toks[-1]] + take(bn)
                    tmm = None
                    for f in range(16):
                        def f_(t, s, f=f, j=j, half=half, bk=bk):
                            return t.matmul(
                                bk[:, :],
                                hsb[:, f * MCH + j * 128: f * MCH + j * 128 + 128],
                                w3sb[:, f * D + half * 512: f * D + half * 512 + 512],
                                start=(f == 0), stop=(f == 15))
                        tmm = top(f_, mmdeps)
                        mmdeps = []
                    last_mm = tmm
                    if e == E - 1:
                        pass

                    def fac(v, s, j=j, e=e, half=half, bk=bk, k=k):
                        sl_ = acc2[k % 2][:, j * D + half * 512:
                                          j * D + half * 512 + 512]
                        return v.scalar_tensor_tensor(
                            sl_, bk[:, :], comb[:, j * 8 + e: j * 8 + e + 1],
                            sl_, OP.mult, OP.add)
                    t_ac = vop(fac, [tmm, acc_tok[(j, half)], comb_tok[j]])
                    use(bn, t_ac)
                    acc_tok[(j, half)] = t_ac
            use("w3sb", last_mm)
            for f in range(16):
                use(f"hsb_f{f}", last_mm)

        if k + 1 < nchunk:
            x1T_next = emit_x1T_reload(k + 1)

        def emit_LN2(k=k, acc_tok=dict(acc_tok), comb_tok=list(comb_tok)):
            _emit_LN2_body(k, acc_tok, comb_tok)
        pend_LN2.append(emit_LN2)
        if k + 1 >= nchunk:
            while pend_LN2:
                pend_LN2.pop(0)()

    P.op("sync", lambda sy, s: sy.nop(),
         deps=[(st_o[0], P.counts[st_o[0]]), (st_o[1], P.counts[st_o[1]]),
               (ld_w3, P.counts[ld_w3]), (ld_w1[0], P.counts[ld_w1[0]]),
               (ld_w1[1], P.counts[ld_w1[1]]), (ld_w2[0], P.counts[ld_w2[0]]),
               (ld_w2[1], P.counts[ld_w2[1]])])

    P.check_feasible()
    sem_es = P.emit(nc)
    sem_es.close()
    es_moe.close()
    es_glob.close()
    return nc


def _host_prep(inputs):
    f32 = np.float32
    bf16 = ml_dtypes.bfloat16
    qk_w = np.asarray(inputs["L0_attn_proj_qk_w"], f32)
    qk_b = np.asarray(inputs["L0_attn_proj_qk_b"], f32)
    v_w = np.asarray(inputs["L0_attn_proj_v_w"], f32)
    v_b = np.asarray(inputs["L0_attn_proj_v_b"], f32)
    in_w = np.asarray(inputs["L0_attn_in_proj_w"], f32)
    in_b = np.asarray(inputs["L0_attn_in_proj_b"], f32)
    op_w = np.asarray(inputs["L0_attn_out_proj_w"], f32)
    op_b = np.asarray(inputs["L0_attn_out_proj_b"], f32)
    Wg = np.asarray(inputs["L0_ffn_Wg"], f32)
    W1 = np.asarray(inputs["L0_ffn_W1"], f32)
    b1 = np.asarray(inputs["L0_ffn_b1"], f32)
    W2 = np.asarray(inputs["L0_ffn_W2"], f32)
    b2 = np.asarray(inputs["L0_ffn_b2"], f32)
    W3 = np.asarray(inputs["L0_ffn_W3"], f32)
    b3 = np.asarray(inputs["L0_ffn_b3"], f32)
    wq, wk, wv = in_w[0:D], in_w[D:2 * D], in_w[2 * D:3 * D]
    bq, bk, bv = in_b[0:D], in_b[D:2 * D], in_b[2 * D:3 * D]
    def aswz(w):
        return np.ascontiguousarray(
            w.T.reshape(8, 128, D).transpose(1, 0, 2).reshape(128, 8 * D))

    def w12swz(W):
        Wt = W.transpose(0, 2, 1)
        return np.ascontiguousarray(
            Wt.reshape(E, 8, 128, 8, 256).transpose(0, 3, 2, 1, 4)
            .reshape(E * 8 * 128, 8 * 256))

    def w3swz(W):
        Wt = W.transpose(0, 2, 1)
        return np.ascontiguousarray(
            Wt.reshape(E, 2, 8, 128, D).transpose(0, 1, 3, 2, 4)
            .reshape(E * 2 * 128, 8 * D))

    return {
        "qkwT": aswz(qk_w),
        "vwT": aswz(v_w),
        "wqT": aswz(wq),
        "wkT": aswz(wk),
        "wvT": aswz(wv),
        "opT": aswz(op_w),
        "qkb_c": np.ascontiguousarray(qk_b.reshape(8, 128).T),
        "vb_c": np.ascontiguousarray(v_b.reshape(8, 128).T),
        "bq_r": bq.reshape(1, D).astype(bf16),
        "bk_r": bk.reshape(1, D).astype(bf16),
        "bv_r": bv.reshape(1, D).astype(bf16),
        "opb_r": op_b.reshape(1, D).astype(bf16),
        "w1T": w12swz(W1),
        "w2T": w12swz(W2),
        "w3T": w3swz(W3),
        "wgT_s": np.ascontiguousarray(
            Wg.T.reshape(8, 128, 8).transpose(1, 0, 2).reshape(128, 64)),
        "b1c": np.ascontiguousarray(
            b1.reshape(8, 16, 128).transpose(2, 0, 1).reshape(128, 128)),
        "b2c": np.ascontiguousarray(
            b2.reshape(8, 16, 128).transpose(2, 0, 1).reshape(128, 128)),
        "b3bf": b3.astype(bf16),
        "idnb": np.eye(128, dtype=bf16),
        "idnf": np.eye(128, dtype=np.float32),
        "onesb": np.ones((1, 512), dtype=bf16),
    }


def _vt_perm():
    p = np.empty(T, dtype=np.int64)
    for c in range(N_ACH):
        for i in range(4):
            for n in range(128):
                p[c * 512 + i * 128 + n] = c * 512 + 4 * n + i
    return p


def kernel(**inputs):
    x = np.asarray(inputs["L0_attn_x"], dtype=np.float32)
    com = _host_prep(inputs)
    xs = x.reshape(N_CORES, 2 * 2048, D)
    perm = _vt_perm()
    in_maps = []
    for c in range(N_CORES):
        m = dict(com)
        m["x"] = np.ascontiguousarray(xs[c][perm])
        in_maps.append(m)
    nc = build_program()
    res = run_bass_kernel_spmd(nc, in_maps, core_ids=list(range(N_CORES)))
    outs = []
    for r in res.results:
        ov = np.asarray(r["out"])
        oreal = np.empty_like(ov)
        oreal[perm] = ov
        outs.append(oreal.reshape(2, 2048, D))
    return np.concatenate(outs, axis=0).astype(np.float32)


if __name__ == "__main__":
    nc = build_program()
    print("build ok")


# version 13
# speedup vs baseline: 1.0926x; 1.0926x over previous
import math
from contextlib import ExitStack

import numpy as np
import ml_dtypes

import concourse.bass as bass
import concourse.mybir as mybir
from concourse.bass_utils import run_bass_kernel_spmd

F32 = mybir.dt.float32
BF16 = mybir.dt.bfloat16
AF = mybir.ActivationFunctionType
OP = mybir.AluOpType
AX = mybir.AxisListType

N_CORES = 8
D = 1024
DFF = 2048
E = 8
SEG = 4
H = 8
HD = 128
SCALE = 1.0 / math.sqrt(HD)
EPS = 1e-5

DEBUG_TAPS = False
PHASES = "AB"
MOE_EXPERTS = 8
MOE_LN2 = True
MOE_CHUNKS = 4
T = 4096
ACH = 512
N_ACH = T // ACH
MCH = 1024
N_MCH = T // MCH


class Plan:

    ENGS = ("sync", "tensor", "vector", "scalar", "gpsimd")

    def __init__(self):
        self.phases = [{e: [] for e in self.ENGS}]
        self.sem_names = []
        self.counts = {}
        self.last_tok = {}

    def sem(self, name):
        self.sem_names.append(name)
        self.counts[name] = 0
        return name

    def phase_break(self):
        self.phases.append({e: [] for e in self.ENGS})

    SELF_EDGE = ("vector", "scalar", "gpsimd")

    def op(self, eng, fn, deps=(), dma_sem=None):
        deps = [d for d in deps if d is not None]
        if dma_sem is not None:
            s, amt = dma_sem, 16
        else:
            s, amt = "s_" + eng, 1
            if eng in self.SELF_EDGE and eng in self.last_tok:
                deps.append(self.last_tok[eng])
        self.counts[s] += amt
        tok = (s, self.counts[s])
        self.phases[-1][eng].append((fn, tuple(deps), s, amt))
        if dma_sem is None:
            self.last_tok[eng] = tok
        return tok

    def check_feasible(self):
        vals = {s: 0 for s in self.counts}
        for ph in self.phases:
            idx = {e: 0 for e in self.ENGS}
            progress = True
            while progress:
                progress = False
                for e in self.ENGS:
                    while idx[e] < len(ph[e]):
                        fn, deps, s, amt = ph[e][idx[e]]
                        if all(vals[ds] >= dv for ds, dv in deps):
                            vals[s] += amt
                            idx[e] += 1
                            progress = True
                        else:
                            break
            if any(idx[e] < len(ph[e]) for e in self.ENGS):
                detail = []
                for e in self.ENGS:
                    if idx[e] < len(ph[e]):
                        fn, deps, s, amt = ph[e][idx[e]]
                        bad = [(ds, dv, vals[ds]) for ds, dv in deps if vals[ds] < dv]
                        detail.append(f"{e}[{idx[e]}] waits {bad}")
                raise RuntimeError("deadlock in plan: " + "; ".join(detail))

    def emit(self, nc):
        es = ExitStack()
        sems = {name: es.enter_context(nc.semaphore(name))
                for name in self.sem_names}
        for ph in self.phases:
            with nc.Block() as block:
                for engname in self.ENGS:
                    if not ph[engname]:
                        continue

                    def body(eng, stream=ph[engname]):
                        last_wait = {}
                        for fn, deps, s, amt in stream:
                            byname = {}
                            for ds, dv in deps:
                                byname[ds] = max(byname.get(ds, 0), dv)
                            for ds, dv in byname.items():
                                if last_wait.get(ds, -1) < dv:
                                    eng.wait_ge(sems[ds], dv)
                                    last_wait[ds] = dv
                            fn(eng, sems).then_inc(sems[s], amt)

                    getattr(block, engname)(body)
        return es


def build_program():
    nc = bass.Bass("TRN2", target_bir_lowering=False)
    P = Plan()
    for s in ("s_sync", "s_tensor", "s_vector", "s_scalar", "s_gpsimd"):
        P.sem(s)
    ld_cs = P.sem("ld_cs")
    ld_cw = P.sem("ld_cw")
    ld_aw = [P.sem(f"ld_aw{i}") for i in range(6)]
    ld_X = P.sem("ld_X")
    ld_xr = [P.sem("ld_xr0"), P.sem("ld_xr1")]
    ld_x1T = P.sem("ld_x1T")
    ld_w1 = [P.sem("ld_w1a"), P.sem("ld_w1b")]
    ld_w2 = [P.sem("ld_w2a"), P.sem("ld_w2b")]
    ld_w3 = P.sem("ld_w3")
    ld_xg = [P.sem("ld_xg0"), P.sem("ld_xg1")]
    st_f = [P.sem("st_f0"), P.sem("st_f1")]
    st_t = [P.sem("st_t0"), P.sem("st_t1")]
    st_o = [P.sem("st_o0"), P.sem("st_o1")]

    guard = {}

    def use(res, tok):
        guard.setdefault(res, []).append(tok)

    def take(res):
        return guard.pop(res, [])

    def vop(fn, deps=()):
        return P.op("vector", fn, list(deps))

    def sop(fn, deps=()):
        return P.op("scalar", fn, list(deps))

    def top(fn, deps=()):
        return P.op("tensor", fn, list(deps))

    def dma(eng, dst, src, deps=(), sem=None):
        return P.op(eng, lambda g, s, d=dst, sr=src: g.dma_start(d, sr),
                    deps=list(deps), dma_sem=sem)

    def din(name, shape, dt=F32):
        return nc.dram_tensor(name, shape, dt, kind="ExternalInput").ap()

    x_h = din("x", [T, D])
    qkwT_h = din("qkwT", [128, 8 * D]); vwT_h = din("vwT", [128, 8 * D])
    wqT_h = din("wqT", [128, 8 * D]); wkT_h = din("wkT", [128, 8 * D])
    wvT_h = din("wvT", [128, 8 * D])
    opT_h = din("opT", [128, 8 * D])
    qkb_c_h = din("qkb_c", [128, 8]); vb_c_h = din("vb_c", [128, 8])
    bq_r_h = din("bq_r", [1, D], BF16); bk_r_h = din("bk_r", [1, D], BF16)
    bv_r_h = din("bv_r", [1, D], BF16); opb_r_h = din("opb_r", [1, D], BF16)
    w1T_h = din("w1T", [E * 8 * 128, 8 * 256])
    w2T_h = din("w2T", [E * 8 * 128, 8 * 256])
    w3T_h = din("w3T", [E * 2 * 128, 8 * D])
    wgT_s_h = din("wgT_s", [128, 64])
    b1c_h = din("b1c", [128, E * 16]); b2c_h = din("b2c", [128, E * 16])
    b3_h = din("b3bf", [E, D], BF16)
    idnb_h = din("idnb", [128, 128], BF16); idnf_h = din("idnf", [128, 128])
    ones_h = din("onesb", [1, 512], BF16)

    out_h = nc.dram_tensor("out", [T, D], F32, kind="ExternalOutput").ap()
    dbg = {}
    if DEBUG_TAPS:
        for nm, shape, dt in (("dbg_X", [128, 4096], F32),
                              ("dbg_xT", [128, 4096], BF16),
                              ("dbg_qkin", [128, 4096], BF16),
                              ("dbg_vin", [128, 4096], BF16),
                              ("dbg_Q", [128, 4096], BF16),
                              ("dbg_K", [128, 4096], BF16),
                              ("dbg_V", [128, 4096], BF16),
                              ("dbg_S", [128, 96], F32),
                              ("dbg_A", [128, 96], F32),
                              ("dbg_o", [128, 1024], F32),
                              ("dbg_oT", [128, 1024], BF16),
                              ("dbg_xr", [128, 1024], F32)):
            dbg[nm] = nc.dram_tensor(nm, shape, dt, kind="ExternalOutput").ap()
    st_dbgs = {nm: P.sem(f"st_dbg{i}") for i, nm in enumerate(
        ("dbg_X", "dbg_xT", "dbg_qkin", "dbg_vin", "dbg_Q", "dbg_K", "dbg_V",
         "dbg_S", "dbg_A", "dbg_o", "dbg_oT", "dbg_xr"))}
    x1f_h = nc.dram_tensor("x1f", [T, D], F32, kind="Internal").ap()
    x1t_h = nc.dram_tensor("x1t", [8, 128, T], BF16, kind="Internal").ap()

    es_glob = ExitStack()

    def sbg(name, shape, dt):
        return es_glob.enter_context(nc.sbuf_tensor(name, shape, dt))

    idnb = sbg("idnb_s", [128, 128], BF16)
    idnf = sbg("idnf_s", [128, 128], F32)
    ones = sbg("ones_s", [1, 512], BF16)
    qkb_c = sbg("qkb_c_s", [128, 8], F32)
    vb_c = sbg("vb_c_s", [128, 8], F32)
    bq_r = sbg("bq_r_s", [1, D], BF16)
    bk_r = sbg("bk_r_s", [1, D], BF16)
    bv_r = sbg("bv_r_s", [1, D], BF16)
    opb_r = sbg("opb_r_s", [1, D], BF16)
    wgT_s = sbg("wgT_s_s", [128, 64], BF16)
    b1c = sbg("b1c_s", [128, E * 16], F32)
    b2c = sbg("b2c_s", [128, E * 16], F32)
    b3s = sbg("b3s_s", [E, D], BF16)

    c_toks = []
    for dst, src_ in ((idnb, idnb_h), (idnf, idnf_h), (ones, ones_h),
                      (bq_r, bq_r_h), (bk_r, bk_r_h), (bv_r, bv_r_h),
                      (opb_r, opb_r_h), (b3s, b3_h)):
        c_toks.append(dma("sync", dst[:, :], src_, sem=ld_cs))
    for dst, src_ in ((qkb_c, qkb_c_h), (vb_c, vb_c_h), (wgT_s, wgT_s_h),
                      (b1c, b1c_h), (b2c, b2c_h)):
        c_toks.append(dma("gpsimd", dst[:, :], src_, sem=ld_cw))
    CONST = [c_toks[-6], c_toks[-1]]

    es_att = ExitStack()

    def sba(name, shape, dt):
        return es_att.enter_context(nc.sbuf_tensor(name, shape, dt))

    qkwT = sba("qkwT_s", [128, 8 * D], BF16)
    vwT = sba("vwT_s", [128, 8 * D], BF16)
    wqT = sba("wqT_s", [128, 8 * D], BF16)
    wkT = sba("wkT_s", [128, 8 * D], BF16)
    wvT = sba("wvT_s", [128, 8 * D], BF16)
    opT = sba("opT_s", [128, 8 * D], BF16)
    Xbf = sba("Xbf", [128, 4 * D], BF16)
    xres = [sba(f"xres{p}", [128, D], F32) for p in range(2)]
    xT = sba("xT", [128, 8 * ACH], BF16)
    qk_in = sba("qk_in", [128, 8 * ACH], BF16)
    v_in = sba("v_in", [128, 8 * ACH], BF16)
    Q = sba("Q", [128, 4 * D], BF16)
    K = sba("K", [128, 4 * D], BF16)
    V = sba("V", [128, 4 * D], BF16)
    Pb = sba("Pb", [128, D], F32)
    Sb = sba("Sb", [128, 96], F32)
    Ebuf = sba("Ebuf", [128, 96], F32)
    Db = sba("Db", [128, 32], F32)
    Rb = sba("Rb", [128, 32], F32)
    Ab = sba("Ab", [128, 96], F32)
    ob = [sba(f"ob{p}", [128, D], F32) for p in range(2)]
    obf = [sba(f"obf{p}", [128, D], BF16) for p in range(2)]
    oT = [sba(f"oT{p}", [128, D], BF16) for p in range(2)]
    xr = [sba(f"xr{p}", [128, D], F32) for p in range(2)]
    x1b = ob
    x1bf = [sba(f"x1bf{p}", [128, D], BF16) for p in range(2)]
    x1ts = obf
    scol = sba("scol", [128, 1], F32)
    negm = sba("negm", [128, 1], F32)
    vars_ = sba("vars", [128, 1], F32)
    v1c = sba("v1c", [128, 1], F32)
    l1c = sba("l1c", [128, 1], F32)
    rstd = sba("rstd", [128, 1], F32)

    pband = [es_att.enter_context(nc.psum_tensor(f"pb{i}", [128, 512], F32))
             for i in range(6)]
    trb = [es_att.enter_context(nc.psum_tensor(f"trb{i}", [128, 1024], BF16))
           for i in range(2)]

    xsrc0 = bass.AP(x_h.tensor, 0, [[D, 128], [128 * D, 4], [1, D]])
    t_xl0 = dma("gpsimd", Xbf[:, :], xsrc0, sem=ld_X)
    aw = []
    for wi, (dst, src_) in enumerate(((qkwT, qkwT_h), (vwT, vwT_h),
                                      (wqT, wqT_h), (wkT, wkT_h),
                                      (wvT, wvT_h), (opT, opT_h))):
        aw.append(dma("gpsimd", dst[:, :], src_, sem=ld_aw[wi]))

    pairs = [(i, jj, [j for j in range(4) if j != i][jj])
             for i in range(4) for jj in range(3)]

    def pcol(i, jj):
        return (i * 3 + jj) * 8

    x1f_sp = []
    x1t_sp = []

    def emit_X1(c):
        if c == 0:
            t_xl = t_xl0
        else:
            xsrc = bass.AP(x_h.tensor, c * ACH * D,
                           [[D, 128], [128 * D, 4], [1, D]])
            t_xl = dma("gpsimd", Xbf[:, :], xsrc, deps=take("Xbf"), sem=ld_X)

        xT_evacs = []
        xT_guard = take("xT")
        last_tr = None
        for i in range(4):
            bk = trb[0]
            bankdeps = [t_xl] + take("tr0") + (CONST if c == 0 and i == 0 else [])
            ttr = None
            for dt in range(8):
                def f(t, s, i=i, dt=dt, bk=bk):
                    return t.transpose(bk[:, dt * 128:(dt + 1) * 128],
                                       Xbf[:, i * D + dt * 128: i * D + (dt + 1) * 128],
                                       idnb[:, :])
                ttr = top(f, bankdeps)
                bankdeps = []
            last_tr = ttr

            def fe(se, s, i=i, bk=bk):
                dst = bass.AP(xT, i * 128, [[8 * ACH, 128], [ACH, 8], [1, 128]])
                src = bass.AP(bk, 0, [[1024, 128], [128, 8], [1, 128]])
                return se.activation(dst, src, AF.Identity)
            tev = sop(fe, [ttr] + xT_guard)
            xT_guard = []
            use("tr0", tev)
            xT_evacs.append(tev)
        use("Xbf", last_tr)

        def proj_fm(wsb, bcols, dst_sb, wtok, wr_guard):
            evacs = []
            for dt in range(8):
                bname = f"pb{dt % 2}"
                bk = pband[dt % 2]
                mmdeps = xT_evacs + [wtok] + take(bname)
                tmm = None
                for kt in range(8):
                    def f(t, s, dt=dt, kt=kt, bk=bk, wsb=wsb):
                        return t.matmul(
                            bk[:, :],
                            wsb[:, kt * D + dt * 128: kt * D + (dt + 1) * 128],
                            xT[:, kt * ACH: (kt + 1) * ACH],
                            start=(kt == 0), stop=(kt == 7))
                    tmm = top(f, mmdeps)
                    mmdeps = []
                use("xT", tmm)

                def fe(se, s, dt=dt, bk=bk, dst_sb=dst_sb, bcols=bcols):
                    return se.activation(dst_sb[:, dt * ACH:(dt + 1) * ACH],
                                         bk[:, :], AF.Identity,
                                         bias=bcols[:, dt:dt + 1])
                ev = sop(fe, [tmm] + wr_guard)
                use(bname, ev)
                evacs.append(ev)
            return evacs

        qk_ev = proj_fm(qkwT, qkb_c, qk_in, aw[0], take("qk_in"))
        v_ev = proj_fm(vwT, vb_c, v_in, aw[1], take("v_in"))
        return {"qk_ev": qk_ev, "v_ev": v_ev}

    def emit_X2_block(c, st, name, i, qkv_ev):
        cfg = {"q": (qk_in, wqT, bq_r, aw[2], st["qk_ev"], (2, 3)),
               "k": (qk_in, wkT, bk_r, aw[3], st["qk_ev"], (4, 5)),
               "v": (v_in, wvT, bv_r, aw[4], st["v_ev"], (2, 3))}
        src_sb, wsb, brow, wtok, src_ev, banks = cfg[name]
        dst = {"q": Q, "k": K, "v": V}[name]
        evs = []
        mmdeps = list(src_ev) + [wtok] + take(f"pb{banks[0]}") \
            + take(f"pb{banks[1]}")
        for kt in range(8):
            for half in range(2):
                def f(t, s, i=i, kt=kt, half=half,
                      src_sb=src_sb, wsb=wsb, bk=pband[banks[half]]):
                    return t.matmul(
                        bk[:, :],
                        src_sb[:, kt * ACH + i * 128: kt * ACH + i * 128 + 128],
                        wsb[:, kt * D + half * 512: kt * D + half * 512 + 512],
                        start=(kt == 0), stop=False)
                top(f, mmdeps)
                mmdeps = []
        for half in range(2):
            def fb(t, s, half=half, bk=pband[banks[half]], brow=brow):
                return t.matmul(bk[:, :], ones[0:1, 0:128],
                                brow[0:1, half * 512: half * 512 + 512],
                                start=False, stop=True)
            tmm = top(fb)
            use("qk_in" if name in ("q", "k") else "v_in", tmm)

            def fe(se, s, i=i, half=half, bk=pband[banks[half]], dst=dst):
                return se.activation(
                    dst[:, i * D + half * 512: i * D + half * 512 + 512],
                    bk[:, :], AF.Identity)
            ev = sop(fe, [tmm] + take(f"{name}w"))
            use(f"pb{banks[half]}", ev)
            evs.append(ev)
        qkv_ev[(name, i)] = evs

    def emit_Y_scores(c, qkv_ev):
        s_last = None
        for (i, jj, j) in pairs:
            def fmul(v, s, i=i, j=j):
                return v.tensor_tensor(Pb[:, :], Q[:, i * D:(i + 1) * D],
                                       K[:, j * D:(j + 1) * D], OP.mult)
            tm = vop(fmul, qkv_ev[("q", i)] + qkv_ev[("k", j)])
            use("qw", tm)
            use("kw", tm)

            def fred(v, s, i=i, jj=jj):
                src = bass.AP(Pb, 0, [[D, 128], [128, 8], [1, 128]])
                return v.tensor_reduce(Sb[:, pcol(i, jj):pcol(i, jj) + 8], src,
                                       AX.X, OP.add)
            s_last = vop(fred, [tm])
        t_exp = sop(lambda se, s: se.activation(Ebuf[:, :], Sb[:, :], AF.Exp,
                                                scale=SCALE), [s_last])
        d_last = None
        for i in range(4):
            def f1(v, s, i=i):
                return v.tensor_tensor(Db[:, i * 8:(i + 1) * 8],
                                       Ebuf[:, pcol(i, 0):pcol(i, 0) + 8],
                                       Ebuf[:, pcol(i, 1):pcol(i, 1) + 8], OP.add)
            vop(f1, [t_exp])

            def f2(v, s, i=i):
                return v.tensor_tensor(Db[:, i * 8:(i + 1) * 8],
                                       Db[:, i * 8:(i + 1) * 8],
                                       Ebuf[:, pcol(i, 2):pcol(i, 2) + 8], OP.add)
            d_last = vop(f2)
        t_rec = vop(lambda v, s: v.reciprocal(Rb[:, :], Db[:, :]), [d_last])
        a_tok = {}
        for (i, jj, j) in pairs:
            def fa(v, s, i=i, jj=jj):
                return v.tensor_tensor(Ab[:, pcol(i, jj):pcol(i, jj) + 8],
                                       Ebuf[:, pcol(i, jj):pcol(i, jj) + 8],
                                       Rb[:, i * 8:(i + 1) * 8], OP.mult)
            a_tok[(i, jj)] = vop(fa, [t_rec])
        return a_tok

    def emit_Y_grp(c, i, qkv_ev, a_tok):
        if True:
            par = i % 2
            obuf = ob[par]
            lastv = None
            first_deps = take(f"ob{par}")
            for jj in range(3):
                j = [jt for jt in range(4) if jt != i][jj]
                for h in range(H):
                    acol = Ab[:, pcol(i, jj) + h: pcol(i, jj) + h + 1]
                    vsl = V[:, j * D + h * 128: j * D + (h + 1) * 128]
                    osl = obuf[:, h * 128:(h + 1) * 128]
                    deps = [a_tok[(i, jj)]] + qkv_ev[("v", j)] + first_deps
                    first_deps = []
                    if jj == 0:
                        def f(v, s, osl=osl, vsl=vsl, acol=acol):
                            return v.tensor_scalar(osl, vsl, acol, None, OP.mult)
                    else:
                        def f(v, s, osl=osl, vsl=vsl, acol=acol):
                            return v.scalar_tensor_tensor(osl, vsl, acol, osl,
                                                          OP.mult, OP.add)
                    lastv = vop(f, deps)
            o_done_i = lastv
            use("vw", o_done_i)

            t_obf = vop(lambda v, s, par=par: v.tensor_copy(obf[par][:, :],
                                                            ob[par][:, :]),
                        [o_done_i] + take(f"obf{par}"))
            bk = trb[1]
            tdeps = [t_obf] + take("tr1")
            ttr = None
            for dt in range(8):
                def f(t, s, par=par, dt=dt, bk=bk):
                    return t.transpose(bk[:, dt * 128:(dt + 1) * 128],
                                       obf[par][:, dt * 128:(dt + 1) * 128],
                                       idnb[:, :])
                ttr = top(f, tdeps)
                tdeps = []
            use(f"obf{par}", ttr)
            t_oT = vop(lambda v, s, par=par, bk=bk: v.tensor_copy(oT[par][:, :],
                                                                  bk[:, :]),
                       [ttr] + take(f"oT{par}"))
            use("tr1", t_oT)

            vt0 = c * ACH + i * 128
            t_xres = dma("sync", xres[par][:, :],
                         bass.AP(x_h.tensor, vt0 * D, [[D, 128], [1, D]]),
                         deps=take(f"xres{par}"), sem=ld_xr[par])
            xr_parts = []
            for half in range(2):
                bname2 = f"pb{half}"
                bk2 = pband[half]
                mmdeps = [t_oT, aw[5]] + take(bname2)
                tmm = None
                for kt in range(8):
                    def f(t, s, kt=kt, half=half, bk2=bk2, par=par):
                        return t.matmul(
                            bk2[:, :],
                            oT[par][:, kt * 128:(kt + 1) * 128],
                            opT[:, kt * D + half * 512: kt * D + half * 512 + 512],
                            start=(kt == 0), stop=False)
                    tmm = top(f, mmdeps)
                    mmdeps = []

                def fb(t, s, half=half, bk2=bk2):
                    return t.matmul(bk2[:, :], ones[0:1, 0:128],
                                    opb_r[0:1, half * 512: half * 512 + 512],
                                    start=False, stop=True)
                tmm = top(fb)
                use(f"oT{par}", tmm)

                def fxr(v, s, par=par, half=half, bk2=bk2):
                    return v.tensor_tensor(
                        xr[par][:, half * 512: half * 512 + 512],
                        xres[par][:, half * 512: half * 512 + 512],
                        bk2[:, :], OP.add)
                tx = vop(fxr, [tmm, t_xres]
                         + take(f"xr{par}" if half == 0 else "_n"))
                use(bname2, tx)
                use(f"xres{par}", tx)
                xr_parts.append(tx)

            t_s = sop(lambda se, s, par=par: se.activation(
                obf[par][:, :], xr[par][:, :], AF.Identity,
                accum_out=scol[:, 0:1]), xr_parts)
            t_nm = vop(lambda v, s: v.tensor_scalar(negm[:, :], scol[:, :],
                                                    -1.0 / D, None, OP.mult),
                       [t_s])
            t_v = sop(lambda se, s, par=par: se.activation(
                obf[par][:, :], xr[par][:, :], AF.Square, bias=negm[:, 0:1],
                accum_out=vars_[:, 0:1]), [t_nm])
            use(f"xr{par}", t_v)
            t_v1 = vop(lambda v, s: v.tensor_scalar(v1c[:, :], vars_[:, :],
                                                    1.0 / D, EPS, OP.mult, OP.add),
                       [t_v])
            t_l = sop(lambda se, s: se.activation(l1c[:, :], v1c[:, :], AF.Ln),
                      [t_v1])
            t_r = sop(lambda se, s: se.activation(rstd[:, :], l1c[:, :], AF.Exp,
                                                  scale=-0.5), [t_l])
            t_x1 = vop(lambda v, s, par=par: v.tensor_scalar(
                x1b[par][:, :], xr[par][:, :], negm[:, 0:1], rstd[:, 0:1],
                OP.add, OP.mult), [t_r] + take(f"ob{par}"))
            use(f"xr{par}", t_x1)
            t_x1bf = vop(lambda v, s, par=par: v.tensor_copy(x1bf[par][:, :],
                                                             x1b[par][:, :]),
                         [t_x1] + take(f"x1bf{par}"))
            t_sp1 = dma("sync", bass.AP(x1f_h.tensor, vt0 * D, [[D, 128], [1, D]]),
                        x1b[par][:, :], deps=[t_x1], sem=st_f[par])
            use(f"ob{par}", t_sp1)
            x1f_sp.append(t_sp1)

            bk3 = trb[1]
            tdeps = [t_x1bf] + take("tr1")
            ttr = None
            for dt in range(8):
                def f(t, s, par=par, dt=dt, bk3=bk3):
                    return t.transpose(bk3[:, dt * 128:(dt + 1) * 128],
                                       x1bf[par][:, dt * 128:(dt + 1) * 128],
                                       idnb[:, :])
                ttr = top(f, tdeps)
                tdeps = []
            use(f"x1bf{par}", ttr)
            t_xe = vop(lambda v, s, par=par, bk3=bk3: v.tensor_copy(
                x1ts[par][:, :], bk3[:, :]), [ttr])
            use("tr1", t_xe)
            dst = bass.AP(x1t_h.tensor, vt0, [[T, 128], [128 * T, 8], [1, 128]])
            src = bass.AP(x1ts[par], 0, [[D, 128], [128, 8], [1, 128]])
            t_sp2 = dma("sync", dst, src, deps=[t_xe], sem=st_t[par])
            use(f"obf{par}", t_sp2)
            x1t_sp.append(t_sp2)

    stX1 = emit_X1(0)
    qkv_cur = {}
    for i in range(4):
        for nm in ("q", "k", "v"):
            emit_X2_block(0, stX1, nm, i, qkv_cur)
    for c in range(N_ACH):
        stX1n = emit_X1(c + 1) if c + 1 < N_ACH else None
        a_tok = emit_Y_scores(c, qkv_cur)
        qkv_nxt = {}
        for i in range(4):
            if stX1n is not None:
                emit_X2_block(c + 1, stX1n, "q", i, qkv_nxt)
                emit_X2_block(c + 1, stX1n, "k", i, qkv_nxt)
            emit_Y_grp(c, i, qkv_cur, a_tok)
        if stX1n is not None:
            for i in range(4):
                emit_X2_block(c + 1, stX1n, "v", i, qkv_nxt)
        qkv_cur = qkv_nxt

    att_done = [(st_f[0], P.counts[st_f[0]]), (st_f[1], P.counts[st_f[1]]),
                (st_t[0], P.counts[st_t[0]]), (st_t[1], P.counts[st_t[1]])]

    es_att.close()
    if PHASES == "A":
        P.op("sync", lambda sy, s: sy.nop(),
             deps=[(st_f[0], P.counts[st_f[0]]), (st_f[1], P.counts[st_f[1]]),
                   (st_t[0], P.counts[st_t[0]]), (st_t[1], P.counts[st_t[1]])])
        P.check_feasible()
        sem_es = P.emit(nc)
        sem_es.close()
        es_glob.close()
        return nc
    P.phase_break()
    guard.clear()

    es_moe = ExitStack()

    def sbm(name, shape, dt):
        return es_moe.enter_context(nc.sbuf_tensor(name, shape, dt))

    x1T = sbm("x1T", [128, 8 * MCH], BF16)
    hsb = sbm("hsb", [128, 16 * MCH], BF16)
    acc2 = [sbm(f"acc{p}", [128, 8 * D], F32) for p in range(2)]
    w3sb = sbm("w3sb", [128, 16 * D], BF16)
    w1f = [sbm(f"w1f{p}", [128, 8 * 256], BF16) for p in range(2)]
    w2f = [sbm(f"w2f{p}", [128, 8 * 256], BF16) for p in range(2)]
    su = [sbm(f"su{p}", [128, MCH], BF16) for p in range(2)]
    gout = sbm("gout", [128, 8], F32)
    m1 = sbm("m1", [128, 1], F32)
    m2 = sbm("m2", [128, 1], F32)
    ind1 = sbm("ind1", [128, 8], F32)
    ind2 = sbm("ind2", [128, 8], F32)
    maskg = sbm("maskg", [128, 8], F32)
    diffg = sbm("diffg", [128, 1], F32)
    w1cg = sbm("w1cg", [128, 1], F32)
    w2cg = sbm("w2cg", [128, 1], F32)
    comb = sbm("comb", [128, 8 * E], F32)
    combT = sbm("combT", [8, 8 * 128], BF16)
    x1g = [sbm(f"x1g{p}", [128, D], F32) for p in range(2)]
    xr2 = [sbm(f"xr2{p}", [128, D], F32) for p in range(2)]
    outg = [sbm(f"outg{p}", [128, D], F32) for p in range(2)]
    junk2 = sbm("junk2", [128, D], BF16)
    sc2 = sbm("sc2", [128, 1], F32)
    nm2 = sbm("nm2", [128, 1], F32)
    vr2 = sbm("vr2", [128, 1], F32)
    v12 = sbm("v12", [128, 1], F32)
    l12 = sbm("l12", [128, 1], F32)
    rs2 = sbm("rs2", [128, 1], F32)

    pm = [es_moe.enter_context(nc.psum_tensor(f"pm{i}", [128, 512], F32))
          for i in range(8)]

    out_toks = []
    nchunk = min(N_MCH, MOE_CHUNKS)

    def emit_x1T_reload(k):
        toks = []
        g1T = (att_done if k == 0 else []) + take("x1T")
        for dt in range(8):
            src = bass.AP(x1t_h.tensor, dt * 128 * T + k * MCH,
                          [[T, 128], [1, MCH]])
            toks.append(dma("sync", x1T[:, dt * MCH:(dt + 1) * MCH], src,
                            deps=g1T, sem=ld_x1T))
        return toks

    def _emit_LN2_body(k, acc_tok, comb_tok):
        def gop(fn, deps=()):
            return P.op("gpsimd", fn, list(deps))

        for j in range(8 if MOE_LN2 else 0):
            g = k * 8 + j
            par = j % 2
            vt0 = g * 128
            src = bass.AP(x1f_h.tensor, vt0 * D, [[D, 128], [1, D]])
            t_xg = dma("sync", x1g[par][:, :], src,
                       deps=att_done[:2] + take(f"x1g{par}"), sem=ld_xg[par])
            t_xr = gop(lambda v, s, j=j, par=par, k=k: v.tensor_tensor(
                xr2[par][:, :], x1g[par][:, :],
                acc2[k % 2][:, j * D:(j + 1) * D], OP.add),
                [t_xg, acc_tok[(j, 0)], acc_tok[(j, 1)]] + take(f"xr2{par}"))
            use(f"x1g{par}", t_xr)
            use(f"combg{j}", t_xr)
            t_ms = sop(lambda se, s, par=par: se.activation(
                junk2[:, :], xr2[par][:, :], AF.Identity,
                accum_out=sc2[:, 0:1]), [t_xr])
            t_nm = gop(lambda v, s: v.tensor_scalar(nm2[:, :], sc2[:, :],
                                                    -1.0 / D, None, OP.mult),
                       [t_ms])
            t_v = sop(lambda se, s, par=par: se.activation(
                junk2[:, :], xr2[par][:, :], AF.Square, bias=nm2[:, 0:1],
                accum_out=vr2[:, 0:1]), [t_nm])
            use(f"xr2{par}", t_v)
            t_v1 = gop(lambda v, s: v.tensor_scalar(v12[:, :], vr2[:, :],
                                                    1.0 / D, EPS, OP.mult,
                                                    OP.add), [t_v])
            t_l = sop(lambda se, s: se.activation(l12[:, :], v12[:, :], AF.Ln),
                      [t_v1])
            t_r = sop(lambda se, s: se.activation(rs2[:, :], l12[:, :], AF.Exp,
                                                  scale=-0.5), [t_l])
            t_o = gop(lambda v, s, par=par: v.tensor_scalar(
                outg[par][:, :], xr2[par][:, :], nm2[:, 0:1], rs2[:, 0:1],
                OP.add, OP.mult), [t_r] + take(f"outg{par}"))
            use(f"xr2{par}", t_o)
            dst = bass.AP(out_h.tensor, vt0 * D, [[D, 128], [1, D]])
            t_st = dma("sync", dst, outg[par][:, :], deps=[t_o], sem=st_o[par])
            use(f"outg{par}", t_st)
            out_toks.append(t_st)

    pend_LN2 = []
    x1T_next = emit_x1T_reload(0)

    for k in range(nchunk):
        x1T_toks = x1T_next

        comb_tok = [None] * 8
        acc_tok = {}
        for j in range(8):
            mmdeps = x1T_toks + CONST + take("pm6")
            tmm = None
            for kt in range(8):
                def f(t, s, j=j, kt=kt):
                    return t.matmul(
                        pm[6][:, 0:8],
                        x1T[:, kt * MCH + j * 128: kt * MCH + j * 128 + 128],
                        wgT_s[:, kt * 8:(kt + 1) * 8],
                        start=(kt == 0), stop=(kt == 7))
                tmm = top(f, mmdeps)
                mmdeps = []
            t_g = vop(lambda v, s: v.tensor_copy(gout[:, :], pm[6][:, 0:8]),
                      [tmm])
            use("pm6", t_g)
            t_m1 = vop(lambda v, s: v.reduce_max(m1[:, :], gout[:, :], AX.X))
            t_i1 = vop(lambda v, s: v.tensor_scalar(ind1[:, :], gout[:, :],
                                                    m1[:, 0:1], None, OP.is_ge))
            t_mk = vop(lambda v, s: v.scalar_tensor_tensor(
                maskg[:, :], ind1[:, :], -1e30, gout[:, :], OP.mult, OP.add))
            t_m2 = vop(lambda v, s: v.reduce_max(m2[:, :], maskg[:, :], AX.X))
            t_i2 = vop(lambda v, s: v.tensor_scalar(ind2[:, :], maskg[:, :],
                                                    m2[:, 0:1], None, OP.is_ge))
            t_df = vop(lambda v, s: v.tensor_tensor(diffg[:, :], m2[:, :],
                                                    m1[:, :], OP.subtract))
            t_w2 = sop(lambda se, s: se.activation(w2cg[:, :], diffg[:, :],
                                                   AF.Sigmoid), [t_df])
            t_w1 = sop(lambda se, s: se.activation(w1cg[:, :], diffg[:, :],
                                                   AF.Sigmoid, scale=-1.0))
            t_c1 = vop(lambda v, s, j=j: v.tensor_scalar(
                comb[:, j * 8:(j + 1) * 8], ind1[:, :], w1cg[:, 0:1], None,
                OP.mult), [t_w1] + take(f"combg{j}"))
            t_cb = vop(lambda v, s, j=j: v.scalar_tensor_tensor(
                comb[:, j * 8:(j + 1) * 8], ind2[:, :], w2cg[:, 0:1],
                comb[:, j * 8:(j + 1) * 8], OP.mult, OP.add))
            comb_tok[j] = t_cb

        def emit_gating_part2():
            for j in range(8):
                def ftr(t, s, j=j):
                    return t.transpose(pm[7][0:8, 0:128],
                                       comb[:, j * 8:(j + 1) * 8], idnf[:, :])
                ttr = top(ftr, [comb_tok[j]] + take("pm7"))

                def fce(v, s, j=j):
                    return v.tensor_copy(combT[:, j * 128:(j + 1) * 128],
                                         pm[7][0:8, 0:128])
                t_ct = vop(fce, [ttr] + take(f"combTg{j}"))
                use("pm7", t_ct)
                for half in range(2):
                    bn = f"pm{half}"

                    def fmm(t, s, j=j, half=half):
                        return t.matmul(pm[half][:, :],
                                        combT[:, j * 128:(j + 1) * 128],
                                        b3s[:, half * 512: half * 512 + 512],
                                        start=True, stop=True)
                    tmm2 = top(fmm, [t_ct] + take(bn))
                    use(f"combTg{j}", tmm2)

                    def fai(v, s, j=j, half=half, k=k):
                        return v.tensor_copy(
                            acc2[k % 2][:, j * D + half * 512:
                                        j * D + half * 512 + 512],
                            pm[half][:, :])
                    t_ai = vop(fai, [tmm2])
                    use(bn, t_ai)
                    acc_tok[(j, half)] = t_ai

        for e in range(MOE_EXPERTS):
            def w_slice_dma(sl, e=e, extra=()):
                off_ = (e * 8 + sl) * 128 * 2048
                src1 = bass.AP(w1T_h.tensor, off_, [[2048, 128], [1, 2048]])
                src2 = bass.AP(w2T_h.tensor, off_, [[2048, 128], [1, 2048]])
                t1 = dma("gpsimd", w1f[sl % 2][:, :], src1,
                         deps=take(f"w1f{sl % 2}") + list(extra),
                         sem=ld_w1[sl % 2])
                t2 = dma("gpsimd", w2f[sl % 2][:, :], src2,
                         deps=take(f"w2f{sl % 2}"), sem=ld_w2[sl % 2])
                return t1, t2

            w1_toks = {}
            w2_toks = {}
            extra0 = att_done if (k == 0 and e == 0) else []
            w1_toks[0], w2_toks[0] = w_slice_dma(0, extra=extra0)
            w1_toks[1], w2_toks[1] = w_slice_dma(1)
            w3_toks = []
            g3 = take("w3sb")
            for hh in range(2):
                src3 = bass.AP(w3T_h.tensor, (e * 2 + hh) * 128 * 8 * D,
                               [[8 * D, 128], [1, 8 * D]])
                w3_toks.append(dma("gpsimd",
                                   w3sb[:, hh * 8 * D:(hh + 1) * 8 * D],
                                   src3, deps=g3, sem=ld_w3))

            h_last = None
            for f in range(16):
                sl = f // 2
                off = (f % 2) * 128
                gbanks = (0, 1) if f % 2 == 0 else (4, 5)
                vbanks = (2, 3) if f % 2 == 0 else (6, 7)
                mmdeps = [w1_toks[sl]] + (x1T_toks if f == 0 and e == 0 else []) \
                    + take(f"pm{gbanks[0]}") + take(f"pm{gbanks[1]}") \
                    + take(f"hsb_f{f}")
                tmm = None
                for kt in range(8):
                    for half in range(2):
                        def f_(t, s, kt=kt, half=half, gb=pm[gbanks[half]],
                               sl=sl, off=off):
                            return t.matmul(
                                gb[:, :],
                                w1f[sl % 2][:, kt * 256 + off: kt * 256 + off + 128],
                                x1T[:, kt * MCH + half * 512: kt * MCH + half * 512 + 512],
                                start=(kt == 0), stop=(kt == 7))
                        tmm = top(f_, mmdeps)
                        mmdeps = []
                se_toks = []
                for half in range(2):
                    def fs(se, s, e=e, f=f, half=half, gb=pm[gbanks[half]],
                           fpar=f % 2):
                        return se.activation(
                            su[fpar][:, half * 512: half * 512 + 512],
                            gb[:, :], AF.Silu,
                            bias=b1c[:, e * 16 + f: e * 16 + f + 1])
                    se_tok = sop(fs, [tmm] + take(f"su{f % 2}_{half}"))
                    use(f"pm{gbanks[half]}", se_tok)
                    se_toks.append(se_tok)
                mmdeps = [w2_toks[sl]] + take(f"pm{vbanks[0]}") \
                    + take(f"pm{vbanks[1]}")
                tmmv = None
                for kt in range(8):
                    for half in range(2):
                        def f_(t, s, kt=kt, half=half, vb=pm[vbanks[half]],
                               sl=sl, off=off):
                            return t.matmul(
                                vb[:, :],
                                w2f[sl % 2][:, kt * 256 + off: kt * 256 + off + 128],
                                x1T[:, kt * MCH + half * 512: kt * MCH + half * 512 + 512],
                                start=(kt == 0), stop=(kt == 7))
                        tmmv = top(f_, mmdeps)
                        mmdeps = []
                for half in range(2):
                    def fh(v, s, e=e, f=f, half=half, vb=pm[vbanks[half]],
                           fpar=f % 2):
                        return v.scalar_tensor_tensor(
                            hsb[:, f * MCH + half * 512: f * MCH + half * 512 + 512],
                            vb[:, :], b2c[:, e * 16 + f: e * 16 + f + 1],
                            su[fpar][:, half * 512: half * 512 + 512],
                            OP.add, OP.mult)
                    th = vop(fh, [tmmv, se_toks[half]])
                    use(f"pm{vbanks[half]}", th)
                    use(f"su{f % 2}_{half}", th)
                    h_last = th
                if f == 2 * sl + 1:
                    use(f"w1f{sl % 2}", h_last)
                    use(f"w2f{sl % 2}", h_last)
                    if sl + 2 < 8:
                        w1_toks[sl + 2], w2_toks[sl + 2] = w_slice_dma(sl + 2)
                if f == 15 and e == E - 1:
                    use("x1T", h_last)

            if e == 0:
                emit_gating_part2()
                while pend_LN2:
                    pend_LN2.pop(0)()

            last_mm = None
            for j in range(8):
                banks = (0, 1) if j % 2 == 0 else (2, 3)
                for half in range(2):
                    bk = pm[banks[half]]
                    bn = f"pm{banks[half]}"
                    mmdeps = [h_last, w3_toks[-1]] + take(bn)
                    tmm = None
                    for f in range(16):
                        def f_(t, s, f=f, j=j, half=half, bk=bk):
                            return t.matmul(
                                bk[:, :],
                                hsb[:, f * MCH + j * 128: f * MCH + j * 128 + 128],
                                w3sb[:, f * D + half * 512: f * D + half * 512 + 512],
                                start=(f == 0), stop=(f == 15))
                        tmm = top(f_, mmdeps)
                        mmdeps = []
                    last_mm = tmm
                    if e == E - 1:
                        pass

                    def fac(v, s, j=j, e=e, half=half, bk=bk, k=k):
                        sl_ = acc2[k % 2][:, j * D + half * 512:
                                          j * D + half * 512 + 512]
                        return v.scalar_tensor_tensor(
                            sl_, bk[:, :], comb[:, j * 8 + e: j * 8 + e + 1],
                            sl_, OP.mult, OP.add)
                    t_ac = vop(fac, [tmm, acc_tok[(j, half)], comb_tok[j]])
                    use(bn, t_ac)
                    acc_tok[(j, half)] = t_ac
            use("w3sb", last_mm)
            for f in range(16):
                use(f"hsb_f{f}", last_mm)

        if k + 1 < nchunk:
            x1T_next = emit_x1T_reload(k + 1)

        def emit_LN2(k=k, acc_tok=dict(acc_tok), comb_tok=list(comb_tok)):
            _emit_LN2_body(k, acc_tok, comb_tok)
        pend_LN2.append(emit_LN2)
        if k + 1 >= nchunk:
            while pend_LN2:
                pend_LN2.pop(0)()

    P.op("sync", lambda sy, s: sy.nop(),
         deps=[(st_o[0], P.counts[st_o[0]]), (st_o[1], P.counts[st_o[1]]),
               (ld_w3, P.counts[ld_w3]), (ld_w1[0], P.counts[ld_w1[0]]),
               (ld_w1[1], P.counts[ld_w1[1]]), (ld_w2[0], P.counts[ld_w2[0]]),
               (ld_w2[1], P.counts[ld_w2[1]])])

    P.check_feasible()
    sem_es = P.emit(nc)
    sem_es.close()
    es_moe.close()
    es_glob.close()
    return nc


def _host_prep(inputs):
    f32 = np.float32
    bf16 = ml_dtypes.bfloat16
    qk_w = np.asarray(inputs["L0_attn_proj_qk_w"], f32)
    qk_b = np.asarray(inputs["L0_attn_proj_qk_b"], f32)
    v_w = np.asarray(inputs["L0_attn_proj_v_w"], f32)
    v_b = np.asarray(inputs["L0_attn_proj_v_b"], f32)
    in_w = np.asarray(inputs["L0_attn_in_proj_w"], f32)
    in_b = np.asarray(inputs["L0_attn_in_proj_b"], f32)
    op_w = np.asarray(inputs["L0_attn_out_proj_w"], f32)
    op_b = np.asarray(inputs["L0_attn_out_proj_b"], f32)
    Wg = np.asarray(inputs["L0_ffn_Wg"], f32)
    W1 = np.asarray(inputs["L0_ffn_W1"], f32)
    b1 = np.asarray(inputs["L0_ffn_b1"], f32)
    W2 = np.asarray(inputs["L0_ffn_W2"], f32)
    b2 = np.asarray(inputs["L0_ffn_b2"], f32)
    W3 = np.asarray(inputs["L0_ffn_W3"], f32)
    b3 = np.asarray(inputs["L0_ffn_b3"], f32)
    wq, wk, wv = in_w[0:D], in_w[D:2 * D], in_w[2 * D:3 * D]
    bq, bk, bv = in_b[0:D], in_b[D:2 * D], in_b[2 * D:3 * D]
    def aswz(w):
        return np.ascontiguousarray(
            w.T.reshape(8, 128, D).transpose(1, 0, 2).reshape(128, 8 * D))

    def w12swz(W):
        Wt = W.transpose(0, 2, 1)
        return np.ascontiguousarray(
            Wt.reshape(E, 8, 128, 8, 256).transpose(0, 3, 2, 1, 4)
            .reshape(E * 8 * 128, 8 * 256))

    def w3swz(W):
        Wt = W.transpose(0, 2, 1)
        return np.ascontiguousarray(
            Wt.reshape(E, 2, 8, 128, D).transpose(0, 1, 3, 2, 4)
            .reshape(E * 2 * 128, 8 * D))

    return {
        "qkwT": aswz(qk_w),
        "vwT": aswz(v_w),
        "wqT": aswz(wq),
        "wkT": aswz(wk),
        "wvT": aswz(wv),
        "opT": aswz(op_w),
        "qkb_c": np.ascontiguousarray(qk_b.reshape(8, 128).T),
        "vb_c": np.ascontiguousarray(v_b.reshape(8, 128).T),
        "bq_r": bq.reshape(1, D).astype(bf16),
        "bk_r": bk.reshape(1, D).astype(bf16),
        "bv_r": bv.reshape(1, D).astype(bf16),
        "opb_r": op_b.reshape(1, D).astype(bf16),
        "w1T": w12swz(W1),
        "w2T": w12swz(W2),
        "w3T": w3swz(W3),
        "wgT_s": np.ascontiguousarray(
            Wg.T.reshape(8, 128, 8).transpose(1, 0, 2).reshape(128, 64)),
        "b1c": np.ascontiguousarray(
            b1.reshape(8, 16, 128).transpose(2, 0, 1).reshape(128, 128)),
        "b2c": np.ascontiguousarray(
            b2.reshape(8, 16, 128).transpose(2, 0, 1).reshape(128, 128)),
        "b3bf": b3.astype(bf16),
        "idnb": np.eye(128, dtype=bf16),
        "idnf": np.eye(128, dtype=np.float32),
        "onesb": np.ones((1, 512), dtype=bf16),
    }


def _vt_perm():
    p = np.empty(T, dtype=np.int64)
    for c in range(N_ACH):
        for i in range(4):
            for n in range(128):
                p[c * 512 + i * 128 + n] = c * 512 + 4 * n + i
    return p


def kernel(**inputs):
    x = np.asarray(inputs["L0_attn_x"], dtype=np.float32)
    com = _host_prep(inputs)
    xs = x.reshape(N_CORES, 2 * 2048, D)
    perm = _vt_perm()
    in_maps = []
    for c in range(N_CORES):
        m = dict(com)
        m["x"] = np.ascontiguousarray(xs[c][perm])
        in_maps.append(m)
    nc = build_program()
    res = run_bass_kernel_spmd(nc, in_maps, core_ids=list(range(N_CORES)))
    outs = []
    for r in res.results:
        ov = np.asarray(r["out"])
        oreal = np.empty_like(ov)
        oreal[perm] = ov
        outs.append(oreal.reshape(2, 2048, D))
    return np.concatenate(outs, axis=0).astype(np.float32)


if __name__ == "__main__":
    nc = build_program()
    print("build ok")
